# version 12
# baseline (speedup 1.0000x reference)
import numpy as np
import ml_dtypes
from contextlib import ExitStack

import concourse.bass as bass
import concourse.mybir as mybir
from concourse.bass_utils import run_bass_kernel_spmd

F32 = mybir.dt.float32
BF16 = mybir.dt.bfloat16
AF = mybir.ActivationFunctionType
ALU = mybir.AluOpType
AX = mybir.AxisListType

NEG = -1.0e30
B, S, DM = 4, 4096, 1024
NH, NKV, HD = 16, 4, 64
WINDOW = 1024
QTOK = 2048
CTX = 3072
DCH = DM // 128
NQT = QTOK // 128
NREL = 9
CTXCH = CTX // 128
DFF = 4096
NFC = DFF // 128
EPS = 1e-5
QCNT = float(DM * DFF)

ENGS = ("tensor", "vector", "scalar", "sync", "gpsimd")


class Sched:

    SEM_NAMES = ("T", "V", "A", "G", "BAR", "LDS", "GW", "XB0", "XB1",
                 "QI0", "QI1", "QO0", "QO1", "W10", "W11", "W12", "W13",
                 "W20", "W21", "W22", "XQ", "OST")

    def __init__(self, nc, es):
        self.nc = nc
        self.sems = {}
        self.c = {}
        self.events = []
        for name in self.SEM_NAMES:
            self.sems[name] = es.enter_context(nc.semaphore(name))
            self.c[name] = 0

    AUTO_SELF = {"vector": "V", "scalar": "A", "gpsimd": "G"}

    def do(self, eng, fn=None, waits=(), incs=()):
        sem = self.AUTO_SELF.get(eng)
        if sem is not None:
            if self.c[sem] > 0:
                waits = ((sem, self.c[sem]),) + tuple(waits)
            if not any(s == sem for s, _ in incs):
                incs = tuple(incs) + ((sem, 1),)
        for s, v in waits:
            assert v <= self.c[s], f"wait {s}>={v} but only {self.c[s]} emitted"
        self.events.append((eng, tuple(waits), fn, tuple(incs)))
        for s, n in incs:
            self.c[s] += n

    def dma(self, qeng, dst, src, sem, waits=()):
        w = tuple(waits)
        if self.c[sem] > 0:
            w = ((sem, self.c[sem]),) + w
        self.do(qeng, lambda e, d=dst, s=src: e.dma_start(d, s), waits=w,
                incs=((sem, 16),))
        return self.c[sem]

    def barrier(self, drain=()):
        for e in ENGS:
            self.do(e, None, incs=(("BAR", 1),))
        tgt = self.c["BAR"]
        dw = tuple((s, self.c[s]) for s in drain)
        for e in ENGS:
            self.do(e, None, waits=(("BAR", tgt),) + dw)

    def flush(self):
        evs, self.events = self.events, []
        if not evs:
            return
        with self.nc.Block() as blk:
            for name in ENGS:
                mine = [ev for ev in evs if ev[0] == name]
                if not mine:
                    continue

                def mk(mine=mine):
                    def f(eng):
                        for _, waits, fn, incs in mine:
                            for s, v in waits:
                                eng.wait_ge(self.sems[s], v)
                            ins = fn(eng) if fn is not None else None
                            used = False
                            for s, n in incs:
                                if ins is not None and not used:
                                    ins.then_inc(self.sems[s], n)
                                    used = True
                                else:
                                    eng.sem_inc(self.sems[s], n)
                    return f

                getattr(blk, name)(mk())


def build_nc(stop_after=None, p2_level=4, q2_on=True, p2_ng=None, skip_exp=False, p2_pieces=None):
    nc = bass.Bass("TRN2", target_bir_lowering=False)

    xctxT = nc.dram_tensor("xctxT", [DM, CTX], F32, kind="ExternalInput").ap()
    wqT = nc.dram_tensor("wqT", [DM, DM], F32, kind="ExternalInput").ap()
    wkTd = nc.dram_tensor("wkTd", [DM, 512], F32, kind="ExternalInput").ap()
    wvT = nc.dram_tensor("wvT", [DM, 256], F32, kind="ExternalInput").ap()
    woT = nc.dram_tensor("woT", [DM, DM], F32, kind="ExternalInput").ap()
    w1T = nc.dram_tensor("w1T", [DM, DFF], F32, kind="ExternalInput").ap()
    w2T = nc.dram_tensor("w2T", [DFF, DM], F32, kind="ExternalInput").ap()
    b1c = nc.dram_tensor("b1c", [128, NFC], F32, kind="ExternalInput").ap()
    b2c = nc.dram_tensor("b2c", [128, DCH], F32, kind="ExternalInput").ap()
    trimask = nc.dram_tensor("trimask", [128, 1024], BF16, kind="ExternalInput").ap()
    identin = nc.dram_tensor("identin", [128, 128], BF16, kind="ExternalInput").ap()
    validrep = nc.dram_tensor("validrep", [CTX, 128], BF16, kind="ExternalInput").ap()
    outT = nc.dram_tensor("outT", [DM, QTOK], F32, kind="ExternalOutput").ap()
    w1q_t = nc.dram_tensor("w1q_t", [NFC, 128, DM], BF16).ap()
    w2q_t = nc.dram_tensor("w2q_t", [DCH, 128, DFF], BF16).ap()

    es = ExitStack()
    with es:
        S = Sched(nc, es)
        sb = lambda name, shape, dt: es.enter_context(nc.sbuf_tensor(name, shape, dt))

        tri_sb = sb("tri_sb", [128, 1024], BF16)
        id_sb = sb("id_sb", [128, 128], BF16)
        b1_sb = sb("b1_sb", [128, NFC], F32)
        b2_sb = sb("b2_sb", [128, DCH], F32)
        ones_bf = sb("ones_bf", [128, 128], BF16)
        ones_f32 = sb("ones_f32", [128, 128], F32)
        qred = sb("qred", [128, 512], F32)
        gpart = sb("gpart", [128, 32], F32)
        gsum = sb("gsum", [128, 2], F32)
        th1 = sb("th1", [128, 1], F32)
        nth1 = sb("nth1", [128, 1], F32)
        th2 = sb("th2", [128, 1], F32)
        nth2 = sb("nth2", [128, 1], F32)
        g1col = sb("g1col", [128, 1], F32)
        g2col = sb("g2col", [128, 1], F32)
        cm30 = sb("cm30", [128, 1], F32)

        w1b = [es.enter_context(nc.sbuf_tensor(f"w1b{i}", [128, DM], BF16,
                                               side="right")) for i in range(3)]
        otn_ctx = ExitStack()
        OTn = otn_ctx.enter_context(
            nc.sbuf_tensor("OTn", [128, DCH * QTOK], BF16, side="right"))
        wo_bf = otn_ctx.enter_context(
            nc.sbuf_tensor("wo_bf", [128, DCH * 1024], BF16, side="right"))

        ld_small = 0
        gd_all = 0

        def _early_out(src_ap):
            v = S.dma("sync", outT.rearrange("(dc p) q -> p dc q", p=128)[:, 0, 0:512],
                      src_ap, "OST")
            S.do("sync", None, waits=(("OST", v),))

        with ExitStack() as p12:
            sb12 = lambda name, shape, dt: p12.enter_context(nc.sbuf_tensor(name, shape, dt))
            KTd = sb12("KTd", [128, 4 * CTX], BF16)
            Vv = sb12("Vv", [128, CTXCH * 256], BF16)
            QT = sb12("QT", [128, DCH * QTOK], BF16)
            vrep_sb = sb12("vrep_sb", [128, CTXCH * 128], BF16)
            qin = [sb12(f"qin{i}", [128, 2048], F32) for i in range(2)]
            qx = sb12("qx", [128, 2048], BF16)
            qout = [sb12(f"qout{i}", [128, 2048], BF16) for i in range(2)]

            with ExitStack() as p1:
                sb1 = lambda name, shape, dt: p1.enter_context(nc.sbuf_tensor(name, shape, dt))
                ps1 = lambda name: p1.enter_context(nc.psum_tensor(name, [128, 512], F32))
                xb = [sb1(f"xb{i}", [128, DCH * 512], BF16) for i in range(2)]
                wq_bf = sb1("wq_bf", [128, DCH * 1024], BF16)
                wkd_bf = sb1("wkd_bf", [128, DCH * 512], BF16)
                wv_bf = sb1("wv_bf", [128, DCH * 256], BF16)
                kps = [ps1(f"kps{i}") for i in range(2)]
                vps = [ps1(f"vps{i}") for i in range(2)]
                qps = [ps1(f"qps{i}") for i in range(2)]
                gps = ps1("gps")

                S.dma("sync", tri_sb[:, :], trimask, "LDS")
                S.dma("sync", id_sb[:, :], identin, "LDS")
                S.dma("sync", b1_sb[:, :], b1c, "LDS")
                S.dma("sync", b2_sb[:, :], b2c, "LDS")
                S.dma("sync", vrep_sb[:].rearrange("p (ca j) -> p ca j", ca=CTXCH),
                      validrep.rearrange("(ca p) j -> p ca j", p=128), "LDS")
                ld_small = S.c["LDS"]
                S.do("vector", lambda e: e.memset(ones_bf[:, :], 1.0), incs=(("V", 1),))
                S.do("vector", lambda e: e.memset(ones_f32[:, :], 1.0), incs=(("V", 1),))
                S.do("vector", lambda e: e.memset(cm30[:, :], -30.0), incs=(("V", 1),))
                v_consts = S.c["V"]

                xctxT_r = xctxT.rearrange("(dc p) c -> p dc c", p=128)

                def emit_xb_dma(cb, wait_t):
                    dst = xb[cb % 2][:].rearrange("p (dc c) -> p dc c", dc=DCH)
                    src = xctxT_r[:, :, cb * 512:(cb + 1) * 512]
                    waits = (("T", wait_t),) if wait_t > 0 else ()
                    return S.dma("gpsimd", dst, src, f"XB{cb % 2}", waits=waits)

                gw_k = S.dma("gpsimd", wkd_bf[:].rearrange("p (dc f) -> p dc f", dc=DCH),
                             wkTd.rearrange("(dc p) f -> p dc f", p=128), "GW")

                w1T_r = w1T.rearrange("(dc p) f -> p dc f", p=128)
                w2T_r = w2T.rearrange("(a p) d -> p a d", p=128)
                p1_steps = []
                for ci in range(16):
                    dc, fh = ci // 2, ci % 2
                    p1_steps.append((w1T_r[:, dc, fh * 2048:(fh + 1) * 2048], 1))
                for ci in range(16):
                    p1_steps.append((w2T_r[:, 2 * ci:2 * ci + 2, :], 2))

                q1_dma_done = [0] * 32
                q1_red_done = [0] * 32
                st1 = {"issued": 0, "reduced": 0}

                def q1_issue():
                    i = st1["issued"]
                    if i >= 32:
                        return
                    if i >= 2:
                        assert q1_red_done[i - 2] > 0
                        waits = (("V", q1_red_done[i - 2]),)
                    else:
                        waits = ()
                    src, nsub = p1_steps[i]
                    dst = qin[i % 2][:, :]
                    if nsub > 1:
                        dst = dst.rearrange("p (a d) -> p a d", a=nsub)
                    q1_dma_done[i] = S.dma("sync", dst, src, f"QI{i % 2}", waits=waits)
                    st1["issued"] += 1

                def q1_reduce():
                    i = st1["reduced"]
                    if i >= st1["issued"] or i >= 32:
                        return
                    S.do("vector",
                         lambda e, i=i: e.tensor_reduce(
                             qred[:, i * 16:(i + 1) * 16],
                             qin[i % 2][:].rearrange("p (a x) -> p a x", a=16),
                             AX.X, ALU.add, apply_absolute_value=True),
                         waits=((f"QI{i % 2}", q1_dma_done[i]),), incs=(("V", 1),))
                    q1_red_done[i] = S.c["V"]
                    st1["reduced"] += 1

                q1_issue()
                q1_issue()

                xb_dma_done = [0] * 8
                kps_evac = [0, 0]
                vps_evac = [0, 0]
                qps_evac = [0, 0]
                xb_dma_done[0] = emit_xb_dma(0, 0)
                gw_v = S.dma("gpsimd", wv_bf[:].rearrange("p (dc f) -> p dc f", dc=DCH),
                             wvT.rearrange("(dc p) f -> p dc f", p=128), "GW")
                xb_dma_done[1] = emit_xb_dma(1, 0)
                gw_q = S.dma("gpsimd", wq_bf[:].rearrange("p (dc f) -> p dc f", dc=DCH),
                             wqT.rearrange("(dc p) f -> p dc f", p=128), "GW")

                for cb in range(6):
                    x = xb[cb % 2]
                    first_wait = ((f"XB{cb % 2}", xb_dma_done[cb]), ("GW", gw_k))
                    for fc in range(4):
                        slot = fc % 2
                        w = (("A", kps_evac[slot]),) + (first_wait if fc < 2 else ())
                        for dc in range(DCH):
                            S.do("tensor",
                                 lambda e, dc=dc, fc=fc, x=x, slot=slot: e.matmul(
                                     kps[slot][:, :],
                                     wkd_bf[:, dc * 512 + fc * 128: dc * 512 + fc * 128 + 128],
                                     x[:, dc * 512:(dc + 1) * 512],
                                     start=(dc == 0), stop=(dc == DCH - 1)),
                                 waits=(w if dc == 0 else ()),
                                 incs=((("T", 1),) if dc == DCH - 1 else ()))
                        t_done = S.c["T"]
                        S.do("scalar",
                             lambda e, fc=fc, cb=cb, slot=slot: e.copy(
                                 KTd[:, fc * CTX + cb * 512: fc * CTX + cb * 512 + 512],
                                 kps[slot][:, :]),
                             waits=(("T", t_done),), incs=(("A", 1),))
                        kps_evac[slot] = S.c["A"]
                    for tc in range(4):
                        slot = tc % 2
                        w = (("A", vps_evac[slot]),) + ((("GW", gw_v),) if cb == 0 and tc == 0 else ())
                        for dc in range(DCH):
                            S.do("tensor",
                                 lambda e, dc=dc, tc=tc, x=x, slot=slot: e.matmul(
                                     vps[slot][:, 0:256],
                                     x[:, dc * 512 + tc * 128: dc * 512 + tc * 128 + 128],
                                     wv_bf[:, dc * 256:(dc + 1) * 256],
                                     start=(dc == 0), stop=(dc == DCH - 1)),
                                 waits=(w if dc == 0 else ()),
                                 incs=((("T", 1),) if dc == DCH - 1 else ()))
                        t_done = S.c["T"]
                        S.do("scalar",
                             lambda e, cb=cb, tc=tc, slot=slot: e.copy(
                                 Vv[:, (cb * 4 + tc) * 256:(cb * 4 + tc) * 256 + 256],
                                 vps[slot][:, 0:256]),
                             waits=(("T", t_done),), incs=(("A", 1),))
                        vps_evac[slot] = S.c["A"]
                        q1_reduce()
                        q1_issue()
                    if cb >= 2:
                        qcol = (cb - 2) * 512
                        for fc in range(DCH):
                            slot = fc % 2
                            w = (("A", qps_evac[slot]),) + ((("GW", gw_q),) if cb == 2 and fc == 0 else ())
                            for dc in range(DCH):
                                S.do("tensor",
                                     lambda e, dc=dc, fc=fc, x=x, slot=slot: e.matmul(
                                         qps[slot][:, :],
                                         wq_bf[:, dc * 1024 + fc * 128: dc * 1024 + fc * 128 + 128],
                                         x[:, dc * 512:(dc + 1) * 512],
                                         start=(dc == 0), stop=(dc == DCH - 1)),
                                     waits=(w if dc == 0 else ()),
                                     incs=((("T", 1),) if dc == DCH - 1 else ()))
                            t_done = S.c["T"]
                            S.do("scalar",
                                 lambda e, fc=fc, qcol=qcol, slot=slot: e.copy(
                                     QT[:, fc * QTOK + qcol: fc * QTOK + qcol + 512],
                                     qps[slot][:, :]),
                                 waits=(("T", t_done),), incs=(("A", 1),))
                            qps_evac[slot] = S.c["A"]
                            q1_reduce()
                            q1_issue()
                    if cb + 2 < 6:
                        xb_dma_done[cb + 2] = emit_xb_dma(cb + 2, S.c["T"])

                while st1["reduced"] < 32:
                    q1_reduce()
                    q1_issue()

                S.do("vector", lambda e: e.tensor_reduce(
                    gpart[:, 0:16],
                    qred[:, 0:256].rearrange("p (a x) -> p a x", a=16),
                    AX.X, ALU.add),
                    waits=(("V", q1_red_done[15]),), incs=(("V", 1),))
                S.do("vector", lambda e: e.tensor_reduce(
                    gpart[:, 16:32],
                    qred[:, 256:512].rearrange("p (a x) -> p a x", a=16),
                    AX.X, ALU.add),
                    waits=(("V", q1_red_done[31]),), incs=(("V", 1),))
                v_g = S.c["V"]
                S.do("tensor", lambda e: e.matmul(gps[:, 0:16], ones_f32[:, :],
                                                  gpart[:, 0:16], start=True, stop=True),
                     waits=(("V", v_g),), incs=(("T", 1),))
                S.do("tensor", lambda e: e.matmul(gps[:, 16:32], ones_f32[:, :],
                                                  gpart[:, 16:32], start=True, stop=True),
                     incs=(("T", 1),))
                t_g = S.c["T"]
                S.do("vector", lambda e: e.tensor_reduce(
                    gsum[:, 0:1], gps[:, 0:16], AX.X, ALU.add),
                    waits=(("T", t_g),), incs=(("V", 1),))
                S.do("vector", lambda e: e.tensor_reduce(
                    gsum[:, 1:2], gps[:, 16:32], AX.X, ALU.add),
                    incs=(("V", 1),))
                S.do("vector", lambda e: e.tensor_scalar(
                    th1[:, :], gsum[:, 0:1], 0.5 / QCNT, 0.5 * EPS, ALU.mult, ALU.add),
                    incs=(("V", 1),))
                S.do("vector", lambda e: e.tensor_scalar(
                    th2[:, :], gsum[:, 1:2], 0.5 / QCNT, 0.5 * EPS, ALU.mult, ALU.add),
                    incs=(("V", 1),))
                S.do("vector", lambda e: e.tensor_single_scalar(
                    nth1[:, :], th1[:, :], -1.0, ALU.mult), incs=(("V", 1),))
                S.do("vector", lambda e: e.tensor_single_scalar(
                    nth2[:, :], th2[:, :], -1.0, ALU.mult), incs=(("V", 1),))
                v_th = S.c["V"]

                S.barrier(drain=("XB0", "XB1", "GW"))
                if stop_after == "P1":
                    _early_out(qred[:, :])
                S.flush()
            if stop_after == "P1":
                return nc

            with ExitStack() as p2:
                sb2 = lambda name, shape, dt: p2.enter_context(nc.sbuf_tensor(name, shape, dt))
                Tst = sb2("Tst", [128, 4608], F32)
                Pst = [sb2(f"Pst{i}", [128, 4608], BF16) for i in range(2)]
                den_sb = sb2("den_sb", [128, 512], F32)
                spsP = [p2.enter_context(nc.psum_tensor(f"spsP{i}", [128, 1024], F32))
                        for i in range(2)]
                pv_ps = [p2.enter_context(nc.psum_tensor(f"pv_ps{i}", [128, 512], F32))
                         for i in range(2)]
                den_ps = [p2.enter_context(nc.psum_tensor(f"den_ps{i}", [128, 512], F32))
                          for i in range(2)]

                w1T_q = w1T.rearrange("(dc p) f -> p dc f", p=128)
                w2T_q = w2T.rearrange("(fh fs p) d -> p fh fs d", fh=2, p=128)
                q2 = [("w1", fc) for fc in range(NFC)] + [("w2", i) for i in range(16)]
                q2_dma_done = [0] * 48
                q2_cmp_done = [0] * 48
                q2_out_done = [0] * 48
                st2 = {"issued": 0, "done": 0}

                def q2_issue():
                    i = st2["issued"]
                    if i >= 48 or not q2_on:
                        return
                    kind, idx = q2[i]
                    if kind == "w1":
                        src = w1T_q[:, :, idx * 128:(idx + 1) * 128]
                        dst = qin[i % 2][:, 0:1024].rearrange("p (a f) -> p a f", a=DCH)
                    else:
                        fh, dc = idx // 8, idx % 8
                        src = w2T_q[:, fh, :, dc * 128:(dc + 1) * 128]
                        dst = qin[i % 2][:, :].rearrange("p (a f) -> p a f", a=16)
                    if i >= 2:
                        assert q2_cmp_done[i - 2] > 0
                        waits = (("V", q2_cmp_done[i - 2]),)
                    else:
                        waits = (("V", v_th),)
                    q2_dma_done[i] = S.dma("sync", dst, src, f"QI{i % 2}", waits=waits)
                    st2["issued"] += 1

                def q2_step():
                    i = st2["done"]
                    if i >= st2["issued"] or i >= 48:
                        return
                    kind, idx = q2[i]
                    n = 1024 if kind == "w1" else 2048
                    neg = nth1 if kind == "w1" else nth2
                    pos = th1 if kind == "w1" else th2
                    waits = [(f"QI{i % 2}", q2_dma_done[i])]
                    if i >= 2:
                        waits.append((f"QO{i % 2}", q2_out_done[i - 2]))
                    S.do("vector",
                         lambda e, i=i, n=n, neg=neg: e.tensor_single_scalar(
                             qx[:, 0:n], qin[i % 2][:, 0:n], neg[:, :], ALU.is_lt),
                         waits=tuple(waits), incs=(("V", 1),))
                    S.do("vector",
                         lambda e, i=i, n=n, pos=pos: e.scalar_tensor_tensor(
                             qout[i % 2][:, 0:n], qin[i % 2][:, 0:n], pos[:, :],
                             qx[:, 0:n], ALU.is_gt, ALU.subtract),
                         incs=(("V", 1),))
                    q2_cmp_done[i] = S.c["V"]
                    if kind == "w1":
                        dst = w1q_t[idx]
                    else:
                        fh, dc = idx // 8, idx % 8
                        dst = w2q_t[dc][:, fh * 2048:(fh + 1) * 2048]
                    q2_out_done[i] = S.dma(
                        "gpsimd", dst, qout[i % 2][:, 0:n], f"QO{i % 2}",
                        waits=(("V", q2_cmp_done[i]),))
                    st2["done"] += 1

                q2_issue()
                q2_issue()

                Pst_r = [p[:].rearrange("p (rel hp par x) -> p par rel hp x",
                                        rel=NREL, hp=2, par=2) for p in Pst]
                den_r = den_sb[:].rearrange("p (hp par x) -> p par hp x", hp=2, par=2)
                OTn_r = OTn[:].rearrange("p (ch qt x) -> p ch qt x", ch=DCH, qt=NQT)

                piece_T = {}
                mask_V = {}
                tanh_done = {}
                exp_done = {}
                pv_done = {}
                norm_done = {}
                NG = NQT * 4

                def stg_off(kb, par):
                    if kb < 8:
                        return (kb // 2) * 1024 + par * 512 + (kb % 2) * 256
                    return 4096 + par * 256

                tanhP = {}

                def emit_scores_acts(g):
                    qt, kv = divmod(g, 4)
                    for kbp in range(5):
                        na = 2 if kbp < 4 else 1
                        edge = kbp == 0 or kbp == 4
                        ps = spsP[kbp % 2]
                        for par in range(2):
                            base = par * 64
                            for a in range(na):
                                kb = 2 * kbp + a
                                ca = qt + kb
                                lhsT = KTd[base:base + 64,
                                           kv * CTX + ca * 128:
                                           kv * CTX + ca * 128 + 128]
                                for he in range(2):
                                    hg = kv * 4 + 2 * he + par
                                    col = par * 512 + a * 256 + he * 128
                                    rhs = QT[base:base + 64,
                                             (hg // 2) * QTOK + qt * 128:
                                             (hg // 2) * QTOK + qt * 128 + 128]
                                    waits = ()
                                    if par == 0 and a == 0 and he == 0:
                                        prev = ((g, kbp - 2) if kbp >= 2
                                                else (g - 1, 4 - kbp))
                                        if prev in tanhP:
                                            waits = (("A", tanhP[prev]),)
                                    first_in_bank = (a == 0) and (he == 0)
                                    last_in_bank = (a == na - 1) and (he == 1) \
                                        and not edge
                                    piece_last = last_in_bank and par == 1
                                    S.do("tensor",
                                         lambda e, ps=ps, col=col, lhsT=lhsT,
                                         rhs=rhs, st=first_in_bank,
                                         sp=last_in_bank: e.matmul(
                                             ps[:, col:col + 128], lhsT, rhs,
                                             start=st, stop=sp),
                                         waits=waits,
                                         incs=((("T", 1),) if piece_last else ()))
                            if edge:
                                moff = 0 if kbp == 0 else 512
                                S.do("tensor",
                                     lambda e, ps=ps, par=par, moff=moff: e.matmul(
                                         ps[:, par * 512: par * 512 + 256],
                                         id_sb[:, :], tri_sb[:, moff:moff + 256],
                                         start=False, stop=True),
                                     incs=((("T", 1),) if par == 1 else ()))
                        t_piece = S.c["T"]
                        w = [("T", t_piece)]
                        if kbp < 4:
                            S.do("scalar",
                                 lambda e, ps=ps, kbp=kbp: e.activation(
                                     Tst[:, kbp * 1024:(kbp + 1) * 1024],
                                     ps[:, 0:1024], AF.Tanh, scale=1.0 / 240.0),
                                 waits=tuple(w), incs=(("A", 1),))
                        else:
                            S.do("scalar",
                                 lambda e, ps=ps: e.activation(
                                     Tst[:, 4096:4608],
                                     ps[:].rearrange("p (par x) -> p par x",
                                                     par=2)[:, :, 0:256],
                                     AF.Tanh, scale=1.0 / 240.0),
                                 waits=tuple(w), incs=(("A", 1),))
                        tanhP[(g, kbp)] = S.c["A"]
                    w = [("A", tanhP[(g, 4)])]
                    if g >= 2 and pv_done.get(g - 2, 0) > 0:
                        w.append(("T", pv_done[g - 2]))
                    if g == 0:
                        w.append(("V", v_consts), )
                    S.do("scalar",
                         lambda e, g=g: e.activation(
                             Pst[g % 2][:, :], Tst[:, :], AF.Exp, bias=cm30[:, :],
                             scale=30.0),
                         waits=tuple(w), incs=(("A", 1),))
                    exp_done[g] = S.c["A"]

                def emit_pv(g):
                    qt, kv = divmod(g, 4)
                    P = Pst[g % 2]
                    pv = pv_ps[g % 2]
                    dn = den_ps[g % 2]
                    for par in range(2):
                        for rel in range(NREL):
                            ca = qt + rel
                            vt = Vv[:, ca * 256 + kv * 64: ca * 256 + kv * 64 + 64]
                            w0 = []
                            if par == 0 and rel == 0:
                                w0.append(("A", exp_done[g]))
                                if g >= 2 and norm_done.get(g - 2, 0) > 0:
                                    w0.append(("V", norm_done[g - 2]))
                            tp = None if par == 0 else (0, 64)
                            out = pv[0:64, 0:256] if par == 0 else pv[64:128, 0:256]
                            rhs = P[:, stg_off(rel, par): stg_off(rel, par) + 256]
                            S.do("tensor",
                                 lambda e, out=out, vt=vt, rhs=rhs, rel=rel,
                                 tp=tp: e.matmul(
                                     out, vt, rhs, start=(rel == 0),
                                     stop=(rel == NREL - 1), tile_position=tp),
                                 waits=tuple(w0))
                    for par in range(2):
                        for rel in range(NREL):
                            ca = qt + rel
                            rhs = P[:, stg_off(rel, par): stg_off(rel, par) + 256]
                            w0 = []
                            if par == 0 and rel == 0 and g == 0:
                                w0.append(("LDS", ld_small))
                            tp = None if par == 0 else (0, 64)
                            out = (dn[0:64, 0:256] if par == 0
                                   else dn[64:128, 0:256])
                            S.do("tensor",
                                 lambda e, ca=ca, rhs=rhs, rel=rel, out=out,
                                 tp=tp: e.matmul(
                                     out, vrep_sb[:, ca * 128: ca * 128 + 64],
                                     rhs, start=(rel == 0),
                                     stop=(rel == NREL - 1), tile_position=tp),
                                 waits=tuple(w0),
                                 incs=((("T", 1),)
                                       if rel == NREL - 1 and par == 1 else ()))
                    pv_done[g] = S.c["T"]

                def emit_norm(g):
                    qt, kv = divmod(g, 4)
                    pv = pv_ps[g % 2]
                    dn = den_ps[g % 2]
                    S.do("vector",
                         lambda e, dn=dn: e.reciprocal(den_sb[:, 0:256],
                                                       dn[:, 0:256]),
                         waits=(("T", pv_done[g]),), incs=(("V", 1),))
                    vr = S.c["V"]
                    for par in range(2):
                        S.do("vector",
                             lambda e, par=par, kv=kv, qt=qt, pv=pv: e.tensor_tensor(
                                 OTn_r[par * 64:par * 64 + 64, 2 * kv:2 * kv + 2, qt],
                                 pv[par * 64:par * 64 + 64, 0:256],
                                 den_sb[par * 64:par * 64 + 64, 0:256],
                                 ALU.mult),
                             waits=((("V", vr),) if par == 0 else ()),
                             incs=((("V", 1),) if par == 1 else ()))
                    norm_done[g] = S.c["V"]

                for g in range(NG if p2_ng is None else p2_ng):
                    emit_scores_acts(g)
                    if g >= 1:
                        emit_pv(g - 1)
                        emit_norm(g - 1)
                    q2_step()
                    q2_issue()
                _ng = NG if p2_ng is None else p2_ng
                emit_pv(_ng - 1)
                emit_norm(_ng - 1)
                while q2_on and st2["done"] < 48:
                    q2_step()
                    q2_issue()
                qo_final = (S.c["QO0"], S.c["QO1"])
                gd_wo = S.dma("gpsimd", wo_bf[:].rearrange("p (dc f) -> p dc f", dc=DCH),
                              woT.rearrange("(dc p) f -> p dc f", p=128), "GW")

                S.barrier(drain=("QI0", "QI1", "QO0", "QO1", "LDS"))
                if stop_after == "P2":
                    _early_out(Tst[:, 0:512] if p2_level < 4 else den_sb[:, :])
                S.flush()
            if stop_after == "P2":
                return nc

        x2 = sb("x2", [128, DCH * QTOK], F32)
        with ExitStack() as p3:
            sb3 = lambda name, shape, dt: p3.enter_context(nc.sbuf_tensor(name, shape, dt))
            xq = sb3("xq", [128, DCH * QTOK], F32)
            t_bf = sb3("t_bf", [128, 512], BF16)
            t_sq = sb3("t_sq", [128, 512], BF16)
            t_bf2 = sb3("t_bf2", [128, 512], BF16)
            t_sq2 = sb3("t_sq2", [128, 512], BF16)
            mean_sb = sb3("mean_sb", [128, 512], F32)
            tmp_a = sb3("tmp_a", [128, 512], F32)
            tmp_b = sb3("tmp_b", [128, 512], F32)
            sig_sb = sb3("sig_sb", [128, 512], F32)
            rstd_sb = sb3("rstd_sb", [128, 512], F32)
            tmp_g = sb3("tmp_g", [128, 512], F32)
            wops = [p3.enter_context(nc.psum_tensor(f"wops{i}", [128, 512], F32))
                    for i in range(2)]
            sum_ps = [p3.enter_context(nc.psum_tensor(f"sum_ps{i}", [128, 512], F32))
                      for i in range(2)]
            sq_ps = [p3.enter_context(nc.psum_tensor(f"sq_ps{i}", [128, 512], F32))
                     for i in range(2)]

            xq_r = xq[:].rearrange("p (dc c) -> p dc c", dc=DCH)
            for half in range(2):
                ld_xq = S.dma("sync", xq_r[:, half * 4:(half + 1) * 4, :],
                              xctxT_r[:, half * 4:(half + 1) * 4, WINDOW:CTX], "XQ")

            wops_evac = [0, 0]
            stats_guard = [0, 0]
            stats_done = [0] * 4
            prev_stats = [0]

            g_guard = [0]

            def ln_tail(slot, dst, col0, stats_wait):
                sm, sq = sum_ps[slot], sq_ps[slot]
                w0 = tuple(stats_wait)
                if g_guard[0] > 0:
                    w0 = w0 + (("G", g_guard[0]),)
                S.do("vector", lambda e: e.tensor_single_scalar(
                    mean_sb[:, :], sm[:, :], 1.0 / DM, ALU.mult),
                    waits=w0, incs=(("V", 1),))
                S.do("vector", lambda e: e.tensor_tensor(
                    tmp_a[:, :], mean_sb[:, :], mean_sb[:, :], ALU.mult),
                    incs=(("V", 1),))
                S.do("vector", lambda e: e.scalar_tensor_tensor(
                    tmp_b[:, :], sq[:, :], 1.0 / DM, tmp_a[:, :],
                    ALU.mult, ALU.subtract), incs=(("V", 1),))
                stats_guard[slot] = S.c["V"]
                S.do("vector", lambda e: e.tensor_single_scalar(
                    tmp_b[:, :], tmp_b[:, :], EPS, ALU.add), incs=(("V", 1),))
                v_var = S.c["V"]
                S.do("scalar", lambda e: e.activation(sig_sb[:, :], tmp_b[:, :], AF.Sqrt),
                     waits=(("V", v_var),), incs=(("A", 1),))
                a_sig = S.c["A"]
                S.do("vector", lambda e: e.reciprocal(rstd_sb[:, :], sig_sb[:, :]),
                     waits=(("A", a_sig),), incs=(("V", 1),))
                v_rstd = S.c["V"]
                for dc in range(DCH):
                    c0 = dc * QTOK + col0
                    S.do("gpsimd", lambda e, c0=c0: e.tensor_tensor(
                        tmp_g[:, :], dst[:, c0:c0 + 512], mean_sb[:, :], ALU.subtract),
                        waits=((("V", v_rstd),) if dc == 0 else ()), incs=(("G", 1),))
                    S.do("gpsimd", lambda e, c0=c0: e.tensor_tensor(
                        dst[:, c0:c0 + 512], tmp_g[:, :], rstd_sb[:, :], ALU.mult),
                        incs=(("G", 1),))
                g_guard[0] = S.c["G"]
                return S.c["G"]

            def emit_wo_qtg(qtg):
                col0 = qtg * 512
                slot = qtg % 2
                sm, sq = sum_ps[slot], sq_ps[slot]
                pend = []

                def flush_stats():
                    hc, v_tiles, bfs = pend.pop(0)
                    w_sum = [("A", v_tiles)]
                    if hc == 0 and stats_guard[slot] > 0:
                        w_sum.append(("V", stats_guard[slot]))
                    tb, ts = bfs
                    S.do("tensor", lambda e, tb=tb: e.matmul(
                        sm[:, :], ones_bf[:, :], tb[:, :],
                        start=(hc == 0), stop=(hc == DCH - 1)),
                        waits=tuple(w_sum))
                    S.do("tensor", lambda e, ts=ts: e.matmul(
                        sq[:, :], ones_bf[:, :], ts[:, :],
                        start=(hc == 0), stop=(hc == DCH - 1)),
                        incs=(("T", 1),))
                    prev_stats[0] = S.c["T"]

                for hc in range(DCH):
                    wslot = hc % 2
                    for jc in range(DCH):
                        w = []
                        if jc == 0:
                            if hc == 0 and qtg == 0:
                                w.append(("GW", gd_wo))
                            if wops_evac[wslot] > 0:
                                w.append(("V", wops_evac[wslot]))
                        S.do("tensor",
                             lambda e, jc=jc, hc=hc, col0=col0, wslot=wslot: e.matmul(
                                 wops[wslot][:, :],
                                 wo_bf[:, jc * 1024 + hc * 128: jc * 1024 + hc * 128 + 128],
                                 OTn[:, jc * QTOK + col0: jc * QTOK + col0 + 512],
                                 start=(jc == 0), stop=(jc == DCH - 1)),
                             waits=tuple(w) if jc == 0 else (),
                             incs=((("T", 1),) if jc == DCH - 1 else ()))
                    t_wo = S.c["T"]
                    S.do("vector",
                         lambda e, hc=hc, col0=col0, wslot=wslot: e.tensor_tensor(
                             x2[:, hc * QTOK + col0: hc * QTOK + col0 + 512],
                             wops[wslot][:, :],
                             xq[:, hc * QTOK + col0: hc * QTOK + col0 + 512], ALU.add),
                         waits=(("T", t_wo), ("XQ", ld_xq)), incs=(("V", 1),))
                    wops_evac[wslot] = S.c["V"]
                    tb = t_bf if hc % 2 == 0 else t_bf2
                    ts = t_sq if hc % 2 == 0 else t_sq2
                    w_cast = [("V", wops_evac[wslot])]
                    if prev_stats[0] > 0:
                        w_cast.append(("T", prev_stats[0]))
                    S.do("scalar", lambda e, hc=hc, col0=col0, tb=tb: e.copy(
                        tb[:, :], x2[:, hc * QTOK + col0: hc * QTOK + col0 + 512]),
                        waits=tuple(w_cast), incs=(("A", 1),))
                    S.do("scalar", lambda e, hc=hc, col0=col0, ts=ts: e.activation(
                        ts[:, :], x2[:, hc * QTOK + col0: hc * QTOK + col0 + 512],
                        AF.Square), incs=(("A", 1),))
                    pend.append((hc, S.c["A"], (tb, ts)))
                    if len(pend) > 1:
                        flush_stats()
                while pend:
                    flush_stats()
                stats_done[qtg] = S.c["T"]

            for qtg in range(4):
                emit_wo_qtg(qtg)
                if qtg >= 1:
                    ln_tail((qtg - 1) % 2, x2, (qtg - 1) * 512,
                            (("T", stats_done[qtg - 1]),))
            ln_tail(3 % 2, x2, 3 * 512, (("T", stats_done[3]),))
            w1_pre = [0, 0]
            for fc in range(2):
                w1_pre[fc] = S.dma(
                    "sync", w1b[fc][:, :], w1q_t[fc], f"W1{fc}",
                    waits=(("QO0", qo_final[0]), ("QO1", qo_final[1])))

            S.barrier(drain=("XQ", "GW"))
            if stop_after == "P3":
                _early_out(x2[:, 0:512])
            S.flush()
        if stop_after == "P3":
            return nc
        otn_ctx.close()

        with ExitStack() as p4:
            sb4 = lambda name, shape, dt: p4.enter_context(nc.sbuf_tensor(name, shape, dt))
            hT = sb4("hT", [128, NFC * 1024], BF16)
            x2bf = sb4("x2bf", [128, DCH * 1024], BF16)
            w2b = [sb4(f"w2b{i}", [128, DFF], BF16) for i in range(3)]
            f_tmp = sb4("f_tmp", [128, 512], F32)
            t_bf = sb4("t_bf4", [128, 512], BF16)
            t_sq = sb4("t_sq4", [128, 512], BF16)
            mean_sb = sb4("mean_sb4", [128, 512], F32)
            tmp_a = sb4("tmp_a4", [128, 512], F32)
            tmp_b = sb4("tmp_b4", [128, 512], F32)
            sig_sb = sb4("sig_sb4", [128, 512], F32)
            rstd_sb = sb4("rstd_sb4", [128, 512], F32)
            tmp_g = sb4("tmp_g4", [128, 512], F32)
            f1ps = [p4.enter_context(nc.psum_tensor(f"f1ps{i}", [128, 512], F32))
                    for i in range(2)]
            f2ps = [p4.enter_context(nc.psum_tensor(f"f2ps{i}", [128, 512], F32))
                    for i in range(2)]
            sum_ps = [p4.enter_context(nc.psum_tensor(f"sum_ps4{i}", [128, 512], F32))
                      for i in range(2)]
            sq_ps = [p4.enter_context(nc.psum_tensor(f"sq_ps4{i}", [128, 512], F32))
                     for i in range(2)]

            w1_dma = [0] * NFC
            w2_dma = [0] * (2 * DCH)
            w1_use = [0, 0, 0]
            w2_use = [0, 0, 0]
            relu_use = [0, 0]
            f2_evac = [0, 0]
            stats_guard4 = [0, 0]
            prev_stats4 = [0]

            g_guard4 = [0]

            def ln_tail4(slot, col0, stats_wait):
                sm, sq = sum_ps[slot], sq_ps[slot]
                w0 = tuple(stats_wait)
                if g_guard4[0] > 0:
                    w0 = w0 + (("G", g_guard4[0]),)
                S.do("vector", lambda e: e.tensor_single_scalar(
                    mean_sb[:, :], sm[:, :], 1.0 / DM, ALU.mult),
                    waits=w0, incs=(("V", 1),))
                S.do("vector", lambda e: e.tensor_tensor(
                    tmp_a[:, :], mean_sb[:, :], mean_sb[:, :], ALU.mult),
                    incs=(("V", 1),))
                S.do("vector", lambda e: e.scalar_tensor_tensor(
                    tmp_b[:, :], sq[:, :], 1.0 / DM, tmp_a[:, :],
                    ALU.mult, ALU.subtract), incs=(("V", 1),))
                stats_guard4[slot] = S.c["V"]
                S.do("vector", lambda e: e.tensor_single_scalar(
                    tmp_b[:, :], tmp_b[:, :], EPS, ALU.add), incs=(("V", 1),))
                v_var = S.c["V"]
                S.do("scalar", lambda e: e.activation(sig_sb[:, :], tmp_b[:, :], AF.Sqrt),
                     waits=(("V", v_var),), incs=(("A", 1),))
                a_sig = S.c["A"]
                S.do("vector", lambda e: e.reciprocal(rstd_sb[:, :], sig_sb[:, :]),
                     waits=(("A", a_sig),), incs=(("V", 1),))
                v_rstd = S.c["V"]
                for dc in range(DCH):
                    c0 = dc * QTOK + col0
                    S.do("gpsimd", lambda e, c0=c0: e.tensor_tensor(
                        tmp_g[:, :], x2[:, c0:c0 + 512], mean_sb[:, :], ALU.subtract),
                        waits=((("V", v_rstd),) if dc == 0 else ()), incs=(("G", 1),))
                    S.do("gpsimd", lambda e, c0=c0: e.tensor_tensor(
                        x2[:, c0:c0 + 512], tmp_g[:, :], rstd_sb[:, :], ALU.mult),
                        incs=(("G", 1),))
                g_guard4[0] = S.c["G"]
                return S.c["G"]

            outT_r = outT.rearrange("(dc p) q -> p dc q", p=128)
            x2_r = x2[:].rearrange("p (dc c) -> p dc c", dc=DCH)
            out_ld = 0

            def issue_w1(fc, extra=()):
                wslot = fc % 3
                w = list(extra)
                if w1_use[wslot] > 0:
                    w.append(("T", w1_use[wslot]))
                w1_dma[fc] = S.dma("sync", w1b[wslot][:, :], w1q_t[fc],
                                   f"W1{wslot}", waits=tuple(w))

            def issue_w2(i2, dc, extra=()):
                wslot = i2 % 3
                w = list(extra)
                if w2_use[wslot] > 0:
                    w.append(("T", w2_use[wslot]))
                w2_dma[i2] = S.dma("sync", w2b[wslot][:, :], w2q_t[dc],
                                   f"W2{wslot}", waits=tuple(w))

            def emit_cast(t, wait_t):
                tcol = t * 1024
                w_cast = (("T", wait_t),) if wait_t > 0 else ()
                for dc in range(DCH):
                    S.do("vector", lambda e, dc=dc, tcol=tcol: e.tensor_copy(
                        x2bf[:, dc * 1024:(dc + 1) * 1024],
                        x2[:, dc * QTOK + tcol: dc * QTOK + tcol + 1024]),
                        waits=(w_cast if dc == 0 else ()),
                        incs=((("V", 1),) if dc == DCH - 1 else ()))
                return S.c["V"]

            cast_ready = emit_cast(0, 0)
            ffn1_end = [0, 0]
            pending_ln = []

            def flush_ln():
                tcol, sdone = pending_ln.pop(0)
                for sg in range(2):
                    v_ln = ln_tail4(sg, tcol + sg * 512, (("T", sdone[sg]),))
                    c0 = tcol + sg * 512
                    S.do("gpsimd", lambda e, c0=c0: e.dma_start(
                        outT_r[:, :, c0:c0 + 512], x2_r[:, :, c0:c0 + 512]),
                        waits=(("G", v_ln), ("OST", S.c["OST"])) if S.c["OST"] > 0
                        else (("G", v_ln),),
                        incs=(("OST", 16),))

            for t in range(2):
                tcol = t * 1024
                x2bf_ready = cast_ready

                if t == 0:
                    w1_dma[0], w1_dma[1] = w1_pre
                    issue_w1(2)
                else:
                    for fc0 in range(3):
                        issue_w1(fc0)
                for fc in range(NFC):
                    wslot = fc % 3
                    for nt in range(2):
                        pslot = (fc * 2 + nt) % 2
                        for dc in range(DCH):
                            w = []
                            if dc == 0:
                                w = [(f"W1{fc % 3}", w1_dma[fc]), ("V", x2bf_ready)]
                                if relu_use[pslot] > 0:
                                    w.append(("A", relu_use[pslot]))
                            S.do("tensor",
                                 lambda e, dc=dc, nt=nt, wslot=wslot, pslot=pslot: e.matmul(
                                     f1ps[pslot][:, :],
                                     w1b[wslot][:, dc * 128:(dc + 1) * 128],
                                     x2bf[:, dc * 1024 + nt * 512: dc * 1024 + nt * 512 + 512],
                                     start=(dc == 0), stop=(dc == DCH - 1)),
                                 waits=tuple(w) if dc == 0 else (),
                                 incs=((("T", 1),) if dc == DCH - 1 else ()))
                        t_f1 = S.c["T"]
                        if nt == 1:
                            w1_use[wslot] = t_f1
                            if fc + 3 < NFC:
                                issue_w1(fc + 3)
                        S.do("scalar",
                             lambda e, fc=fc, nt=nt, pslot=pslot: e.activation(
                                 hT[:, fc * 1024 + nt * 512: fc * 1024 + nt * 512 + 512],
                                 f1ps[pslot][:, :], AF.Relu,
                                 bias=b1_sb[:, fc:fc + 1], scale=0.5),
                             waits=(("T", t_f1),), incs=(("A", 1),))
                        relu_use[pslot] = S.c["A"]
                a_relu_all = S.c["A"]
                ffn1_end[t] = S.c["T"]
                if t == 0:
                    cast_ready = emit_cast(1, ffn1_end[0])
                if t == 1:
                    flush_ln()

                stats_done4 = [0, 0]
                for nt in range(2):
                    if nt == 0:
                        for j in range(3):
                            issue_w2(j, j)
                    for dc in range(DCH):
                        gi = nt * DCH + dc
                        wslot = gi % 3
                        pslot = gi % 2
                        for fc in range(NFC):
                            w = []
                            if fc == 0:
                                w = [(f"W2{gi % 3}", w2_dma[gi]), ("A", a_relu_all)]
                                if f2_evac[pslot] > 0:
                                    w.append(("V", f2_evac[pslot]))
                            S.do("tensor",
                                 lambda e, fc=fc, nt=nt, wslot=wslot, pslot=pslot: e.matmul(
                                     f2ps[pslot][:, :],
                                     w2b[wslot][:, fc * 128:(fc + 1) * 128],
                                     hT[:, fc * 1024 + nt * 512: fc * 1024 + nt * 512 + 512],
                                     start=(fc == 0), stop=(fc == NFC - 1)),
                                 waits=tuple(w) if fc == 0 else (),
                                 incs=((("T", 1),) if fc == NFC - 1 else ()))
                        t_f2 = S.c["T"]
                        w2_use[wslot] = t_f2
                        if gi + 3 < 2 * DCH:
                            issue_w2(gi + 3, (gi + 3) % DCH)
                        S.do("vector",
                             lambda e, dc=dc, pslot=pslot: e.tensor_scalar(
                                 f_tmp[:, :], f2ps[pslot][:, :], 0.5,
                                 b2_sb[:, dc:dc + 1], ALU.mult, ALU.add),
                             waits=(("T", t_f2),), incs=(("V", 1),))
                        c0 = dc * QTOK + tcol + nt * 512
                        S.do("vector", lambda e, c0=c0: e.tensor_tensor(
                            x2[:, c0:c0 + 512], f_tmp[:, :], x2[:, c0:c0 + 512], ALU.add),
                            incs=(("V", 1),))
                        f2_evac[pslot] = S.c["V"]
                        w_cast4 = [("V", f2_evac[pslot])]
                        if prev_stats4[0] > 0:
                            w_cast4.append(("T", prev_stats4[0]))
                        S.do("scalar", lambda e, c0=c0: e.copy(
                            t_bf[:, :], x2[:, c0:c0 + 512]),
                            waits=tuple(w_cast4), incs=(("A", 1),))
                        S.do("scalar", lambda e, c0=c0: e.activation(
                            t_sq[:, :], x2[:, c0:c0 + 512], AF.Square),
                            incs=(("A", 1),))
                        v_tiles = S.c["A"]
                        w_sum = [("A", v_tiles)]
                        if dc == 0 and stats_guard4[nt] > 0:
                            w_sum.append(("V", stats_guard4[nt]))
                        S.do("tensor", lambda e, dc=dc, nt=nt: e.matmul(
                            sum_ps[nt][:, :], ones_bf[:, :], t_bf[:, :],
                            start=(dc == 0), stop=(dc == DCH - 1)),
                            waits=tuple(w_sum))
                        S.do("tensor", lambda e, dc=dc, nt=nt: e.matmul(
                            sq_ps[nt][:, :], ones_bf[:, :], t_sq[:, :],
                            start=(dc == 0), stop=(dc == DCH - 1)),
                            incs=(("T", 1),))
                        prev_stats4[0] = S.c["T"]
                        stats_done4[nt] = S.c["T"]
                    if t == 1 and nt == 0:
                        v_ln = ln_tail4(0, tcol, (("T", stats_done4[0]),))
                        S.do("gpsimd", lambda e: e.dma_start(
                            outT_r[:, :, tcol:tcol + 512],
                            x2_r[:, :, tcol:tcol + 512]),
                            waits=(("G", v_ln), ("OST", S.c["OST"])) if S.c["OST"] > 0
                            else (("G", v_ln),),
                            incs=(("OST", 16),))

                if t == 0:
                    pending_ln.append((tcol, tuple(stats_done4)))
                else:
                    v_ln = ln_tail4(1, tcol + 512, (("T", stats_done4[1]),))
                    S.do("gpsimd", lambda e: e.dma_start(
                        outT_r[:, :, tcol + 512:tcol + 1024],
                        x2_r[:, :, tcol + 512:tcol + 1024]),
                        waits=(("G", v_ln), ("OST", S.c["OST"])),
                        incs=(("OST", 16),))

            S.do("gpsimd", None, waits=(("OST", S.c["OST"]),))
            S.flush()

    return nc


_NC = None


def _get_nc():
    global _NC
    if _NC is None:
        _NC = build_nc()
    return _NC


def _host_inputs(inputs):
    x = np.asarray(inputs["L0_attn_x"], dtype=np.float32)
    wq = np.asarray(inputs["L0_attn_Wq"], dtype=np.float32)
    wk = np.asarray(inputs["L0_attn_Wk"], dtype=np.float32)
    wv = np.asarray(inputs["L0_attn_Wv"], dtype=np.float32)
    wo = np.asarray(inputs["L0_attn_Wo"], dtype=np.float32)
    w1 = np.asarray(inputs["L0_ffn_W1"], dtype=np.float32)
    b1 = np.asarray(inputs["L0_ffn_b1"], dtype=np.float32)
    w2 = np.asarray(inputs["L0_ffn_W2"], dtype=np.float32)
    b2 = np.asarray(inputs["L0_ffn_b2"], dtype=np.float32)

    wqT = np.ascontiguousarray(wq.T)
    wkT = wk.T
    wkTd = np.ascontiguousarray(
        np.repeat(wkT.reshape(DM, NKV, HD), 2, axis=1).reshape(DM, 512))
    wvT = np.ascontiguousarray(wv.T)
    woT = np.ascontiguousarray(wo.T)
    w1T = np.ascontiguousarray(w1.T)
    w2T = np.ascontiguousarray(w2.T)
    b1c = np.ascontiguousarray(b1.reshape(NFC, 128).T)
    b2c = np.ascontiguousarray(b2.reshape(DCH, 128).T)

    u = np.arange(128)[:, None]
    qi = np.arange(128)[None, :]
    m0 = np.where(u <= qi, NEG, 0.0).astype(np.float32)
    m8 = np.where(u > qi, NEG, 0.0).astype(np.float32)
    trim = np.ascontiguousarray(
        np.concatenate([np.tile(m0, (1, 4)), np.tile(m8, (1, 4))],
                       axis=1)).astype(ml_dtypes.bfloat16)
    ident = np.eye(128, dtype=ml_dtypes.bfloat16)

    shared = dict(wqT=wqT, wkTd=wkTd, wvT=wvT, woT=woT, w1T=w1T, w2T=w2T,
                  b1c=b1c, b2c=b2c, trimask=trim, identin=ident)

    in_maps = []
    for c in range(8):
        b, h = c // 2, c % 2
        qstart = h * QTOK
        xc = np.zeros((CTX, DM), dtype=np.float32)
        lo = qstart - WINDOW
        if lo < 0:
            xc[-lo:, :] = x[b, 0:qstart + QTOK, :]
        else:
            xc[:, :] = x[b, lo:qstart + QTOK, :]
        xctxT = np.ascontiguousarray(xc.T)
        valid = np.ones((CTX, 128), dtype=ml_dtypes.bfloat16)
        if lo < 0:
            valid[0:-lo, :] = 0
        m = dict(shared)
        m["xctxT"] = xctxT
        m["validrep"] = valid
        in_maps.append(m)
    return in_maps


def kernel(**inputs):
    nc = _get_nc()
    in_maps = _host_inputs(inputs)
    res = run_bass_kernel_spmd(nc, in_maps, core_ids=list(range(8)))
    out = np.empty((B, S, DM), dtype=np.float32)
    for c in range(8):
        b, h = c // 2, c % 2
        out[b, h * QTOK:(h + 1) * QTOK, :] = res.results[c]["outT"].T
    return out
